# version 01 (final)
# baseline (speedup 1.0000x reference)
import math
from contextlib import ExitStack

import numpy as np
import ml_dtypes

import concourse.bass as bass
import concourse.mybir as mybir
from concourse.bass_utils import run_bass_kernel_spmd

F32 = mybir.dt.float32
BF16 = mybir.dt.bfloat16
AF = mybir.ActivationFunctionType
ALU = mybir.AluOpType
AX = mybir.AxisListType

NCORES = 8
B = 2
S = 4096
D = 4096
T = 1024
NBLK = 8
KC = D // 128
HALO = 4

ENGINES = ("tensor", "vector", "scalar", "sync", "gpsimd")


def own_blocks(c):
    return list(range(8 * c, 8 * c + 8))


def blk_batch_j(g):
    return g // 32, g % 32


class Em:

    def __init__(self, nc, ctx):
        self.nc = nc
        self.ctx = ctx
        self.ops = {e: [] for e in ENGINES}
        self.sems = {}
        self.cnt = {}

    def sem(self, name):
        if name not in self.sems:
            self.sems[name] = self.ctx.enter_context(self.nc.semaphore(name))
            self.cnt[name] = 0
        return self.sems[name]

    def bump(self, name, k):
        self.sem(name)
        self.cnt[name] += k
        return self.cnt[name]

    def val(self, name):
        self.sem(name)
        return self.cnt[name]

    def op(self, eng, fn=None, wait=None, inc=None, after_prev=False, chain=None):
        if chain is None:
            chain = eng in ("vector", "scalar", "gpsimd")
        waits = []
        if wait:
            for s, v in wait:
                self.sem(s)
                waits.append((s, v))
        if (chain or after_prev) and self.ops[eng]:
            pfn, pwaits, pinc, pdma = self.ops[eng][-1]
            if pinc is None:
                ch = "__ch_" + eng
                k = 16 if pdma else 1
                self.bump(ch, k)
                pinc = (ch, k)
                self.ops[eng][-1] = (pfn, pwaits, pinc, pdma)
            waits.append((pinc[0], self.cnt[pinc[0]]))
        inc_t = None
        if inc:
            s, k = inc
            self.bump(s, k)
            inc_t = (s, k)
        self.ops[eng].append((fn, waits, inc_t, False))

    def op_dma(self, eng, fn, wait=None, inc=None):
        waits = []
        if wait:
            for s, v in wait:
                self.sem(s)
                waits.append((s, v))
        inc_t = None
        if inc:
            s, k = inc
            self.bump(s, k)
            inc_t = (s, k)
        self.ops[eng].append((fn, waits, inc_t, True))

    def flush(self):
        live = [e for e in ENGINES if self.ops[e]]
        if not live:
            return
        with self.nc.Block() as block:
            for e in live:
                ops = self.ops[e]

                def body(eng, ops=ops):
                    for fn, waits, inc_t, _isdma in ops:
                        for s, v in waits:
                            eng.wait_ge(self.sems[s], v)
                        ins = fn(eng) if fn is not None else None
                        if inc_t is not None:
                            s, k = inc_t
                            if ins is None:
                                ins = eng.engine_nop() if hasattr(eng, "engine_nop") else eng.nop()
                            ins.then_inc(self.sems[s], k)

                getattr(block, e)(body)
        self.ops = {e: [] for e in ENGINES}


def dma(em, eng, out_ap, in_ap, sem=None, wait=None):
    em.op_dma(eng, lambda e, o=out_ap, i=in_ap: e.dma_start(out=o, in_=i),
              wait=wait, inc=(sem, 16) if sem else None)



def emit_group(em, eng, thunks, first_wait=None, last_inc=None):
    n = len(thunks)
    for i, th in enumerate(thunks):
        em.op(eng, th, wait=first_wait if i == 0 else None,
              inc=last_inc if i == n - 1 else None)

def ensure_consts(em, st):
    nc = em.nc
    if "ones_col_bf" in st:
        return
    st["ones_col_bf"] = em.ctx.enter_context(nc.sbuf_tensor("ones_col_bf", [128, 1], BF16))
    st["ones_row_bf"] = em.ctx.enter_context(nc.sbuf_tensor("ones_row_bf", [1, 512], BF16))
    st["ones_row_f"] = em.ctx.enter_context(nc.sbuf_tensor("ones_row_f", [1, 128], F32))
    em.op("vector", lambda e: e.memset(st["ones_col_bf"][:], 1.0))
    em.op("vector", lambda e: e.memset(st["ones_row_bf"][:], 1.0))
    em.op("vector", lambda e: e.memset(st["ones_row_f"][:], 1.0))


def load_shadow(em, shadow, xT_dram, tag):
    for k in range(KC):
        dma(em, "gpsimd", shadow[:, k * T:(k + 1) * T],
            xT_dram[k * 128:(k + 1) * 128, :], sem=tag + "sh")
    em.op("gpsimd", wait=[(tag + "sh", em.val(tag + "sh"))], chain=False)


def proj_residual_ln(em, st, tag, wTt_dram, shadow, xT_dram, xoutT_dram,
                     extra_mm=None, bias_dram=None, first_wait_extra=None,
                     dbg=None, kc_in=KC):
    nc = em.nc
    noc = wTt_dram.shape[0]
    n_out = noc * 128
    with ExitStack() as ctx:
        sb = lambda n, sh, dt: ctx.enter_context(nc.sbuf_tensor(tag + n, sh, dt))
        ps = lambda n, sh, dt: ctx.enter_context(nc.psum_tensor(tag + n, sh, dt))
        wbuf = sb("w", [128, kc_in * 128], BF16)
        xbuf = sb("x", [128, T], F32)
        ybuf = sb("y", [128, T], F32)
        ycast = sb("yc", [128, T], BF16)
        ysq = sb("ys", [128, T], BF16)
        spsum = ps("s", [128, T], F32)
        stat_s = [ps(f"ss{h}", [1, 512], F32) for h in range(2)]
        stat_q = [ps(f"sq{h}", [1, 512], F32) for h in range(2)]
        bias_sb = None
        if bias_dram is not None:
            bias_sb = sb("bias", [1, n_out], BF16)
            dma(em, "gpsimd", bias_sb[:], bias_dram.rearrange("(a o) -> a o", a=1), sem=tag + "bi")

        ydram = nc.dram_tensor(tag + "ydram", [n_out, T], F32, kind="Internal").ap()

        for j in range(noc):
            em.op_dma("gpsimd", lambda e, j=j: e.dma_start(out=wbuf[:], in_=wTt_dram[j]),
                  wait=[(tag + "mmg", j)] if j > 0 else None, inc=(tag + "wd", 16))
            em.op_dma("sync", lambda e, j=j: e.dma_start(out=xbuf[:], in_=xT_dram[j * 128:(j + 1) * 128, :]),
                  wait=[(tag + "yr", j)] if j > 0 else None, inc=(tag + "xd", 16))

            first_wait = [(tag + "wd", 16 * (j + 1))]
            if j > 0:
                first_wait.append((tag + "yr", j))
            elif first_wait_extra:
                first_wait.extend(first_wait_extra)
            if bias_dram is not None and j == 0:
                first_wait.append((tag + "bi", 16))
            thunks = []
            for h in range(2):
                for k in range(kc_in):
                    thunks.append(lambda e, h=h, k=k: e.matmul(
                        spsum[:, h * 512:(h + 1) * 512],
                        wbuf[:, k * 128:(k + 1) * 128],
                        shadow[:, k * T + h * 512:k * T + h * 512 + 512],
                        start=(k == 0), stop=False, skip_group_check=True))
                if bias_sb is not None:
                    thunks.append(lambda e, h=h, j=j: e.matmul(
                        spsum[:, h * 512:(h + 1) * 512],
                        bias_sb[:, j * 128:(j + 1) * 128],
                        st["ones_row_bf"][:],
                        start=False, stop=False, skip_group_check=True))
            if extra_mm is not None:
                thunks.extend(extra_mm(j, spsum))
            emit_group(em, "tensor", thunks, first_wait=first_wait,
                       last_inc=(tag + "mmg", 1))

            w = [(tag + "mmg", j + 1), (tag + "xd", 16 * (j + 1))]
            if j > 0:
                w += [(tag + "ydd", 16 * j), (tag + "sqd", j)]
            em.op("vector", lambda e: e.tensor_add(ybuf[:], xbuf[:], spsum[:]), wait=w)
            em.op("vector", lambda e: e.tensor_copy(ycast[:], ybuf[:]),
                  wait=[(tag + "smg", j)] if j > 0 else None, inc=(tag + "yr", 1),
                  after_prev=True)
            em.op("scalar", lambda e: e.activation(ysq[:], ybuf[:], AF.Square),
                  wait=[(tag + "yr", j + 1)] + ([(tag + "smg", j)] if j > 0 else []),
                  inc=(tag + "sqd", 1))
            sthunks = []
            for h in range(2):
                sthunks.append(lambda e, h=h, j=j: e.matmul(
                    stat_s[h][:], st["ones_col_bf"][:], ycast[:, h * 512:(h + 1) * 512],
                    start=(j == 0), stop=False, skip_group_check=True))
                sthunks.append(lambda e, h=h, j=j: e.matmul(
                    stat_q[h][:], st["ones_col_bf"][:], ysq[:, h * 512:(h + 1) * 512],
                    start=(j == 0), stop=False, skip_group_check=True))
            emit_group(em, "tensor", sthunks,
                       first_wait=[(tag + "yr", j + 1), (tag + "sqd", j + 1)],
                       last_inc=(tag + "smg", 1))
            em.op_dma("sync", lambda e, j=j: e.dma_start(out=ydram[j * 128:(j + 1) * 128, :], in_=ybuf[:]),
                  wait=[(tag + "yr", j + 1)], inc=(tag + "ydd", 16))

        em.op("sync", wait=[(tag + "ydd", 16 * noc), (tag + "smg", noc)])
        em.flush()

        mrow = sb("mrow", [1, T], F32)
        vrow = sb("vrow", [1, T], F32)
        trow = sb("trow", [1, T], F32)
        mb = sb("mb", [128, T], F32)
        rb = sb("rb", [128, T], F32)
        inv_n = 1.0 / n_out
        for h in range(2):
            em.op("vector", lambda e, h=h: e.tensor_scalar_mul(
                mrow[:, h * 512:(h + 1) * 512], stat_s[h][:], inv_n))
            em.op("vector", lambda e, h=h: e.tensor_scalar_mul(
                vrow[:, h * 512:(h + 1) * 512], stat_q[h][:], inv_n))
        em.op("vector", lambda e: e.tensor_mul(trow[:], mrow[:], mrow[:]))
        em.op("vector", lambda e: e.tensor_sub(vrow[:], vrow[:], trow[:]), after_prev=True)
        em.op("vector", lambda e: e.tensor_scalar_add(vrow[:], vrow[:], 1e-5),
              after_prev=True, inc=(tag + "vre", 1))
        em.op("scalar", lambda e: e.activation(trow[:], vrow[:], AF.Sqrt),
              wait=[(tag + "vre", 1)], inc=(tag + "vsq", 1))
        em.op("vector", lambda e: e.reciprocal(vrow[:], trow[:]),
              wait=[(tag + "vsq", 1)])
        em.flush()
        bps = ps("bp", [128, T], F32)
        for src_, dst in ((mrow, mb), (vrow, rb)):
            emit_group(em, "tensor", [
                lambda e, h=h, src_=src_: e.matmul(
                    bps[:, h * 512:(h + 1) * 512], st["ones_row_f"][:],
                    src_[:, h * 512:(h + 1) * 512], start=True, stop=True,
                    skip_group_check=True)
                for h in range(2)], last_inc=(tag + "bc", 1))
            em.op("vector", lambda e, dst=dst: e.tensor_copy(dst[:], bps[:]),
                  wait=[(tag + "bc", em.val(tag + "bc"))])
            em.flush()

        for j in range(noc):
            em.op_dma("sync", lambda e, j=j: e.dma_start(out=xbuf[:], in_=ydram[j * 128:(j + 1) * 128, :]),
                  wait=[(tag + "apv", j)] if j > 0 else None, inc=(tag + "yin", 16))
            em.op("vector", lambda e: e.tensor_sub(ybuf[:], xbuf[:], mb[:]),
                  wait=[(tag + "yin", 16 * (j + 1))] + ([(tag + "apd", 16 * j)] if j > 0 else []))
            em.op("vector", lambda e: e.tensor_mul(ybuf[:], ybuf[:], rb[:]),
                  inc=(tag + "apv", 1), after_prev=True)
            em.op_dma("sync", lambda e, j=j: e.dma_start(out=xoutT_dram[j * 128:(j + 1) * 128, :], in_=ybuf[:]),
                  wait=[(tag + "apv", j + 1)], inc=(tag + "apd", 16))
        em.op("sync", wait=[(tag + "apd", 16 * noc)])
        if dbg is not None:
            dma(em, "sync", dbg["y0"], ydram[0:128, :], sem=tag + "dbg")
            dma(em, "sync", dbg["mrow"], mrow[:], sem=tag + "dbg")
            dma(em, "sync", dbg["rrow"], vrow[:], sem=tag + "dbg")
            em.op("sync", wait=[(tag + "dbg", em.val(tag + "dbg"))])
        em.flush()


def build_l0_ffn(debug=False):
    nc = bass.Bass("TRN2", target_bir_lowering=False, debug=False)
    xT = nc.dram_tensor("xT", [D, T], F32, kind="ExternalInput").ap()
    wTt = nc.dram_tensor("wTt", [KC, 128, KC * 128], F32, kind="ExternalInput").ap()
    aTt = nc.dram_tensor("aTt", [128, KC * 128], F32, kind="ExternalInput").ap()
    bT = nc.dram_tensor("bT", [128, D], F32, kind="ExternalInput").ap()
    bias = nc.dram_tensor("bias", [D], F32, kind="ExternalInput").ap()
    lw = nc.dram_tensor("lw", [8], F32, kind="ExternalInput").ap()
    e8 = nc.dram_tensor("e8", [8, 128], F32, kind="ExternalInput").ap()
    xoutT = nc.dram_tensor("xoutT", [D, T], F32, kind="ExternalOutput").ap()
    dbg = None
    if debug:
        dbg = {"y0": nc.dram_tensor("dbg_y0", [128, T], F32, kind="ExternalOutput").ap(),
               "mrow": nc.dram_tensor("dbg_mrow", [1, T], F32, kind="ExternalOutput").ap(),
               "rrow": nc.dram_tensor("dbg_rrow", [1, T], F32, kind="ExternalOutput").ap()}

    with ExitStack() as ctx:
        em = Em(nc, ctx)
        st = {}
        shadow = ctx.enter_context(nc.sbuf_tensor("shadow", [128, KC * T], BF16))
        aTs = ctx.enter_context(nc.sbuf_tensor("aTs", [128, KC * 128], BF16))
        bTs = ctx.enter_context(nc.sbuf_tensor("bTs", [128, D], BF16))
        lws = ctx.enter_context(nc.sbuf_tensor("lws", [8, 1], F32))
        e8s = ctx.enter_context(nc.sbuf_tensor("e8s", [8, 128], F32))
        hsT = ctx.enter_context(nc.sbuf_tensor("hsT", [128, T], BF16))
        hsF = ctx.enter_context(nc.sbuf_tensor("hsF", [128, T], F32))
        lwx = ctx.enter_context(nc.sbuf_tensor("lwx", [128, 1], F32))

        ensure_consts(em, st)
        load_shadow(em, shadow, xT, "l")
        dma(em, "gpsimd", aTs[:], aTt, sem="la")
        dma(em, "gpsimd", bTs[:], bT, sem="lb")
        dma(em, "sync", lws[:], lw.rearrange("(e o) -> e o", o=1), sem="lw")
        dma(em, "sync", e8s[:], e8, sem="le")
        em.flush()

        with ExitStack() as c2:
            hp = c2.enter_context(nc.psum_tensor("hp", [128, T], F32))
            lp = c2.enter_context(nc.psum_tensor("lp", [128, 1], F32))
            thunks = [lambda e: e.matmul(lp[:], e8s[:], lws[:], start=True,
                                         stop=True, skip_group_check=True)]
            for h in range(2):
                for k in range(KC):
                    thunks.append(lambda e, h=h, k=k: e.matmul(
                        hp[:, h * 512:(h + 1) * 512],
                        aTs[:, k * 128:(k + 1) * 128],
                        shadow[:, k * T + h * 512:k * T + h * 512 + 512],
                        start=(k == 0), stop=False, skip_group_check=True))
            emit_group(em, "tensor", thunks,
                       first_wait=[("lw", 16), ("le", 16), ("la", 16),
                                   ("lsh", em.val("lsh"))],
                       last_inc=("hmm", 1))
            em.op("vector", lambda e: e.tensor_copy(lwx[:], lp[:]), wait=[("hmm", 1)])
            em.op("vector", lambda e: e.tensor_scalar_mul(hsF[:], hp[:], lwx[:]),
                  after_prev=True)
            em.op("vector", lambda e: e.tensor_copy(hsT[:], hsF[:]),
                  inc=("hdone", 1), after_prev=True)
            em.flush()

        def lora_extra(j, spsum):
            out = []
            for h in range(2):
                out.append(lambda e, h=h, j=j: e.matmul(
                    spsum[:, h * 512:(h + 1) * 512],
                    bTs[:, j * 128:(j + 1) * 128],
                    hsT[:, h * 512:(h + 1) * 512],
                    start=False, stop=False, skip_group_check=True))
            return out

        proj_residual_ln(em, st, "f", wTt, shadow, xT, xoutT,
                         extra_mm=lora_extra, bias_dram=bias,
                         first_wait_extra=[("hdone", 1), ("lb", 16)], dbg=dbg)
    return nc



def tile_lhsT(w_lhsT):
    d_in, n_out = w_lhsT.shape
    kc = d_in // 128
    noc = n_out // 128
    w4 = w_lhsT.reshape(kc, 128, noc, 128).transpose(2, 1, 0, 3)
    return np.ascontiguousarray(w4.reshape(noc, 128, kc * 128))


def shard_xT(x_full):
    out = []
    for c in range(NCORES):
        cols = []
        for g in own_blocks(c):
            b, j = blk_batch_j(g)
            cols.append(x_full[b, 128 * j:128 * j + 128, :])
        xc = np.concatenate(cols, axis=0)
        out.append(np.ascontiguousarray(xc.T))
    return out


def unshard_xT(parts):
    x = np.zeros((B, S, D), dtype=np.float32)
    for c in range(NCORES):
        xc = parts[c].T
        for i, g in enumerate(own_blocks(c)):
            b, j = blk_batch_j(g)
            x[b, 128 * j:128 * j + 128, :] = xc[128 * i:128 * (i + 1), :]
    return x


def host_e8(scale):
    e = np.zeros((8, 128), dtype=np.float32)
    for n in range(8):
        e[n, n * 16:(n + 1) * 16] = scale
    return e


def run_l0_ffn(xT_parts, inputs):
    nc = build_l0_ffn()
    wTt = tile_lhsT(np.ascontiguousarray(inputs["L0_ffn_W"].T))
    aTt = np.ascontiguousarray(
        np.asarray(inputs["L0_ffn_lora_A"]).transpose(2, 0, 1).reshape(KC, 128, 128)
        .transpose(1, 0, 2).reshape(128, KC * 128))
    bT = np.ascontiguousarray(np.asarray(inputs["L0_ffn_lora_B"]).transpose(0, 2, 1).reshape(128, D))
    e8 = host_e8(2.0)
    in_maps = [{"xT": xT_parts[c], "wTt": wTt, "aTt": aTt, "bT": bT,
                "bias": np.asarray(inputs["L0_ffn_bias"]),
                "lw": np.asarray(inputs["L0_ffn_lora_weights"]),
                "e8": e8} for c in range(NCORES)]
    res = run_bass_kernel_spmd(nc, in_maps, core_ids=list(range(NCORES)))
    return [res.results[c]["xoutT"] for c in range(NCORES)]


def build_l1_ffn(debug=False):
    nc = bass.Bass("TRN2", target_bir_lowering=False, debug=False)
    xT = nc.dram_tensor("xT", [D, T], F32, kind="ExternalInput").ap()
    wTt = nc.dram_tensor("wTt", [KC, 128, KC * 128], F32, kind="ExternalInput").ap()
    aTt = nc.dram_tensor("aTt", [128, KC * 128], F32, kind="ExternalInput").ap()
    bT = nc.dram_tensor("bT", [128, D], F32, kind="ExternalInput").ap()
    bias = nc.dram_tensor("bias", [D], F32, kind="ExternalInput").ap()
    wrT = nc.dram_tensor("wrT", [128, KC * 8], F32, kind="ExternalInput").ap()
    e8f = nc.dram_tensor("e8f", [8, 128], F32, kind="ExternalInput").ap()
    idf = nc.dram_tensor("idf", [128, 128], F32, kind="ExternalInput").ap()
    xoutT = nc.dram_tensor("xoutT", [D, T], F32, kind="ExternalOutput").ap()
    dbg = None
    if debug:
        dbg = {"y0": nc.dram_tensor("dbg_y0", [128, T], F32, kind="ExternalOutput").ap(),
               "mrow": nc.dram_tensor("dbg_mrow", [1, T], F32, kind="ExternalOutput").ap(),
               "rrow": nc.dram_tensor("dbg_rrow", [1, T], F32, kind="ExternalOutput").ap()}
        dbg["w8"] = nc.dram_tensor("dbg_w8", [128, 8 * 8], F32, kind="ExternalOutput").ap()

    with ExitStack() as ctx:
        em = Em(nc, ctx)
        st = {}
        shadow = ctx.enter_context(nc.sbuf_tensor("shadow", [128, KC * T], BF16))
        aTs = ctx.enter_context(nc.sbuf_tensor("aTs", [128, KC * 128], BF16))
        bTs = ctx.enter_context(nc.sbuf_tensor("bTs", [128, D], BF16))
        wrTs = ctx.enter_context(nc.sbuf_tensor("wrTs", [128, KC * 8], BF16))
        e8s = ctx.enter_context(nc.sbuf_tensor("e8s", [8, 128], F32))
        idfs = ctx.enter_context(nc.sbuf_tensor("idfs", [128, 128], F32))
        hsT = ctx.enter_context(nc.sbuf_tensor("hsT", [128, T], BF16))
        hsF = ctx.enter_context(nc.sbuf_tensor("hsF", [128, T], F32))
        lgF = ctx.enter_context(nc.sbuf_tensor("lgF", [8, T], F32))
        wtok = ctx.enter_context(nc.sbuf_tensor("wtok", [128, 8 * 8], F32))
        wTs = ctx.enter_context(nc.sbuf_tensor("wTs", [8, T], F32))
        wxs = ctx.enter_context(nc.sbuf_tensor("wxs", [128, T], F32))
        rt = ctx.enter_context(nc.sbuf_tensor("rt", [128, 8 * 6], F32))

        ensure_consts(em, st)
        load_shadow(em, shadow, xT, "l")
        dma(em, "gpsimd", aTs[:], aTt, sem="la")
        dma(em, "gpsimd", bTs[:], bT, sem="lb")
        dma(em, "gpsimd", wrTs[:], wrT, sem="lr")
        dma(em, "sync", e8s[:], e8f, sem="le")
        dma(em, "sync", idfs[:], idf, sem="li")
        em.flush()

        with ExitStack() as c2:
            lgp = c2.enter_context(nc.psum_tensor("lgp", [8, T], F32))
            tpp = c2.enter_context(nc.psum_tensor("tpp", [128, 8], F32))
            wpp = c2.enter_context(nc.psum_tensor("wpp", [8, T], F32))
            wxp = c2.enter_context(nc.psum_tensor("wxp", [128, T], F32))
            thunks = []
            for h in range(2):
                for k in range(KC):
                    thunks.append(lambda e, h=h, k=k: e.matmul(
                        lgp[:, h * 512:(h + 1) * 512],
                        wrTs[:, k * 8:(k + 1) * 8],
                        shadow[:, k * T + h * 512:k * T + h * 512 + 512],
                        start=(k == 0), stop=(k == KC - 1), skip_group_check=True))
            emit_group(em, "tensor", thunks,
                       first_wait=[("lr", 16), ("li", 16), ("le", 16),
                                   ("lsh", em.val("lsh"))],
                       last_inc=("lgmm", 1))
            em.op("vector", lambda e: e.tensor_copy(lgF[:], lgp[:]),
                  wait=[("lgmm", 1)], inc=("lgc", 1))
            em.flush()

            for tt in range(8):
                em.op("tensor", lambda e, tt=tt: e.transpose(
                    tpp[:], lgF[:, tt * 128:(tt + 1) * 128], idfs[0:8, 0:8]),
                    wait=[("wtv", em.val("wtv"))] if tt > 0 else None,
                    inc=("tp1", 1))
                lt = rt[:, 0:8]
                m1 = rt[:, 8:9]
                nm1 = rt[:, 9:10]
                p8 = rt[:, 10:18]
                msk = rt[:, 18:26]
                m2 = rt[:, 26:27]
                pm = rt[:, 27:35]
                dn = rt[:, 35:36]
                em.op("vector", lambda e: e.tensor_copy(lt, tpp[:]),
                      wait=[("tp1", em.val("tp1"))])
                em.op("vector", lambda e: e.reduce_max(m1, lt, axis=AX.X))
                em.op("vector", lambda e: e.reduce_max(nm1, lt, axis=AX.X, negate=True),
                      inc=("nm1r", 1))
                em.op("scalar", lambda e: e.activation(p8, lt, AF.Exp, bias=nm1),
                      wait=[("nm1r", em.val("nm1r"))], inc=("expd", 1))
                em.op("vector", lambda e: e.tensor_single_scalar(msk, lt, m1, ALU.is_ge))
                em.op("vector", lambda e: e.scalar_tensor_tensor(msk, msk, -1e30, lt, ALU.mult, ALU.add))
                em.op("vector", lambda e: e.reduce_max(m2, msk, axis=AX.X))
                em.op("vector", lambda e: e.tensor_single_scalar(pm, lt, m2, ALU.is_ge))
                em.op("vector", lambda e: e.tensor_mul(pm, pm, p8),
                      wait=[("expd", em.val("expd"))])
                em.op("vector", lambda e: e.reduce_sum(dn, pm, axis=AX.X))
                em.op("vector", lambda e: e.reciprocal(dn, dn))
                em.op("vector", lambda e, tt=tt: e.tensor_scalar_mul(
                    wtok[:, tt * 8:(tt + 1) * 8], pm, dn), inc=("wtv", 1))
                em.op("tensor", lambda e, tt=tt: e.transpose(
                    wpp[:, tt * 128:(tt + 1) * 128], wtok[:, tt * 8:(tt + 1) * 8], idfs[:]),
                    wait=[("wtv", em.val("wtv"))], inc=("tp2", 1))
            em.op("vector", lambda e: e.tensor_copy(wTs[:], wpp[:]),
                  wait=[("tp2", em.val("tp2"))], inc=("wtc", 1))
            em.flush()
            if dbg is not None:
                dma(em, "sync", dbg["w8"], wtok[:], sem="dw8")
                em.op("sync", wait=[("dw8", 16)])

            emit_group(em, "tensor", [
                lambda e, h=h: e.matmul(wxp[:, h * 512:(h + 1) * 512], e8s[:],
                                        wTs[:, h * 512:(h + 1) * 512],
                                        start=True, stop=True, skip_group_check=True)
                for h in range(2)], first_wait=[("wtc", 1)], last_inc=("wxm", 1))
            em.op("vector", lambda e: e.tensor_copy(wxs[:], wxp[:]), wait=[("wxm", 1)],
                  inc=("wxc", 1))
            hthunks = []
            for h in range(2):
                for k in range(KC):
                    hthunks.append(lambda e, h=h, k=k: e.matmul(
                        wxp[:, h * 512:(h + 1) * 512],
                        aTs[:, k * 128:(k + 1) * 128],
                        shadow[:, k * T + h * 512:k * T + h * 512 + 512],
                        start=(k == 0), stop=(k == KC - 1), skip_group_check=True))
            emit_group(em, "tensor", hthunks,
                       first_wait=[("la", 16), ("wxc", 1)],
                       last_inc=("hmm", 1))
            em.op("vector", lambda e: e.tensor_mul(hsF[:], wxp[:], wxs[:]),
                  wait=[("hmm", 1)])
            em.op("vector", lambda e: e.tensor_copy(hsT[:], hsF[:]), inc=("hdone", 1))
            em.flush()

        def lora_extra(j, spsum):
            out = []
            for h in range(2):
                out.append(lambda e, h=h, j=j: e.matmul(
                    spsum[:, h * 512:(h + 1) * 512],
                    bTs[:, j * 128:(j + 1) * 128],
                    hsT[:, h * 512:(h + 1) * 512],
                    start=False, stop=False, skip_group_check=True))
            return out

        proj_residual_ln(em, st, "f", wTt, shadow, xT, xoutT,
                         extra_mm=lora_extra, bias_dram=bias,
                         first_wait_extra=[("hdone", 1), ("lb", 16)], dbg=dbg)
    return nc


def host_router_tiles(W_router):
    wrT = np.ascontiguousarray(np.asarray(W_router).T)
    return np.ascontiguousarray(wrT.reshape(KC, 128, 8).transpose(1, 0, 2).reshape(128, KC * 8))


def build_l1_qkv():
    nc = bass.Bass("TRN2", target_bir_lowering=False, debug=False)
    xT = nc.dram_tensor("xT", [D, T], F32, kind="ExternalInput").ap()
    wqTt = nc.dram_tensor("wqTt", [KC, 128, KC * 128], F32, kind="ExternalInput").ap()
    wkvTt = nc.dram_tensor("wkvTt", [2, 128, KC * 128], F32, kind="ExternalInput").ap()
    idf = nc.dram_tensor("idf", [128, 128], F32, kind="ExternalInput").ap()
    qT = nc.dram_tensor("qT", [D, T], F32, kind="ExternalOutput").ap()
    kT = nc.dram_tensor("kT", [128, T], F32, kind="ExternalOutput").ap()
    V = nc.dram_tensor("V", [T, 128], F32, kind="ExternalOutput").ap()

    with ExitStack() as ctx:
        em = Em(nc, ctx)
        st = {}
        shadow = ctx.enter_context(nc.sbuf_tensor("shadow", [128, KC * T], BF16))
        wbuf = [ctx.enter_context(nc.sbuf_tensor(f"wbuf{i}", [128, KC * 128], BF16)) for i in range(2)]
        obuf = [ctx.enter_context(nc.sbuf_tensor(f"obuf{i}", [128, T], F32)) for i in range(2)]
        vbb = ctx.enter_context(nc.sbuf_tensor("vbb", [128, T], BF16))
        vtb = ctx.enter_context(nc.sbuf_tensor("vtb", [128, T], F32))
        idb = ctx.enter_context(nc.sbuf_tensor("idb", [128, 128], BF16))
        qp = ctx.enter_context(nc.psum_tensor("qp", [128, T], F32))
        vtp = ctx.enter_context(nc.psum_tensor("vtp", [128, 128], BF16))

        ensure_consts(em, st)
        load_shadow(em, shadow, xT, "l")
        dma(em, "gpsimd", idb[:], idf, sem="li")
        em.flush()

        def mm_chunk(j, wsrc, first_wait, out_dst, scale, tag):
            pj = j % 2
            em.op_dma("gpsimd", lambda e: e.dma_start(out=wbuf[pj][:], in_=wsrc),
                      wait=[("qmm", j - 1)] if j >= 2 else None,
                      inc=(f"qwd{pj}", 16))
            th = []
            for h in range(2):
                for k in range(KC):
                    th.append(lambda e, h=h, k=k: e.matmul(
                        qp[:, h * 512:(h + 1) * 512],
                        wbuf[pj][:, k * 128:(k + 1) * 128],
                        shadow[:, k * T + h * 512:k * T + h * 512 + 512],
                        start=(k == 0), stop=(k == KC - 1), skip_group_check=True))
            fw = [(f"qwd{pj}", 16 * (j // 2 + 1)), ("lsh", em.val("lsh"))]
            if j >= 1:
                fw.append(("qev", j))
            fw += first_wait or []
            emit_group(em, "tensor", th, first_wait=fw, last_inc=("qmm", 1))
            em.op("vector", lambda e: e.tensor_scalar_mul(obuf[pj][:], qp[:], scale),
                  wait=[("qmm", j + 1)] + ([(f"{tag}st{pj}", 16 * (j // 2))] if j >= 2 else []),
                  inc=("qev", 1))
            em.op_dma("sync", lambda e: e.dma_start(out=out_dst, in_=obuf[pj][:]),
                      wait=[("qev", j + 1)], inc=(f"{tag}st{pj}", 16))

        for j in range(KC):
            mm_chunk(j, wqTt[j], None, qT[j * 128:(j + 1) * 128, :], 128.0 ** -0.5, "q")
        mm_chunk(KC, wkvTt[0], None, kT[:], 1.0, "q")
        j = KC + 1
        pj = j % 2
        em.op_dma("gpsimd", lambda e: e.dma_start(out=wbuf[pj][:], in_=wkvTt[1]),
                  wait=[("qmm", j - 1)], inc=(f"qwd{pj}", 16))
        th = []
        for h in range(2):
            for k in range(KC):
                th.append(lambda e, h=h, k=k: e.matmul(
                    qp[:, h * 512:(h + 1) * 512],
                    wbuf[pj][:, k * 128:(k + 1) * 128],
                    shadow[:, k * T + h * 512:k * T + h * 512 + 512],
                    start=(k == 0), stop=(k == KC - 1), skip_group_check=True))
        emit_group(em, "tensor", th,
                   first_wait=[(f"qwd{pj}", 16 * (j // 2 + 1)), ("qev", j)],
                   last_inc=("qmm", 1))
        em.op("vector", lambda e: e.tensor_copy(vbb[:], qp[:]),
              wait=[("qmm", j + 1)], inc=("vbc", 1))
        em.flush()
        for t in range(8):
            em.op("tensor", lambda e, t=t: e.transpose(
                vtp[:], vbb[:, t * 128:(t + 1) * 128], idb[:]),
                wait=[("li", 16), ("vbc", 1), ("vtc", em.val("vtc"))], inc=("vtt", 1))
            em.op("vector", lambda e, t=t: e.tensor_copy(vtb[:, t * 128:(t + 1) * 128], vtp[:]),
                  wait=[("vtt", em.val("vtt"))], inc=("vtc", 1))
        em.flush()
        dma(em, "sync", V.rearrange("(a p) d -> p a d", p=128),
            vtb[:].rearrange("p (a d) -> p a d", d=128), sem="vst")
        em.op("sync", wait=[("vst", 16), ("qst0", em.val("qst0")),
                            ("qst1", em.val("qst1"))])
        em.flush()
    return nc


def build_attn(nh, hd, nkv, debug=False):
    nc = bass.Bass("TRN2", target_bir_lowering=False, debug=False)
    n_ks = (nkv * hd) // 128
    vw = nkv * hd
    kc_out = (nh * hd) // 128
    qT = nc.dram_tensor("qT", [nh * hd, T], F32, kind="ExternalInput").ap()
    kq = nc.dram_tensor("kq", [NBLK, 128, n_ks * S], F32, kind="ExternalInput").ap()
    vq = nc.dram_tensor("vq", [NBLK, 128, 32 * vw], F32, kind="ExternalInput").ap()
    crow = nc.dram_tensor("crow", [NBLK, nh, S], F32, kind="ExternalInput").ap()
    aprt = nc.dram_tensor("aprt", [128, nh], F32, kind="ExternalInput").ap()
    tri = nc.dram_tensor("tri", [128, nh * 128], F32, kind="ExternalInput").ap()
    idf = nc.dram_tensor("idf", [128, 128], F32, kind="ExternalInput").ap()
    xT = nc.dram_tensor("xT", [D, T], F32, kind="ExternalInput").ap()
    woTt = nc.dram_tensor("woTt", [KC, 128, kc_out * 128], F32, kind="ExternalInput").ap()
    xoutT = nc.dram_tensor("xoutT", [D, T], F32, kind="ExternalOutput").ap()
    dbg = None
    if debug:
        dbg = {"y0": nc.dram_tensor("dbg_y0", [128, T], F32, kind="ExternalOutput").ap(),
               "mrow": nc.dram_tensor("dbg_mrow", [1, T], F32, kind="ExternalOutput").ap(),
               "rrow": nc.dram_tensor("dbg_rrow", [1, T], F32, kind="ExternalOutput").ap(),
               "ao": nc.dram_tensor("dbg_ao", [128, T], F32, kind="ExternalOutput").ap()}

    with ExitStack() as ctx:
        em = Em(nc, ctx)
        st = {}
        sbt = lambda n, sh, dt: ctx.enter_context(nc.sbuf_tensor(n, sh, dt))
        attn_sh = sbt("attn_sh", [128, kc_out * T], BF16)
        ksb = sbt("ksb", [128, n_ks * S], BF16)
        vsb = sbt("vsb", [128, 32 * vw], BF16)
        qsb = [sbt(f"qsb{i}", [128, 128], BF16) for i in range(2)]
        crb = [sbt(f"crb{i}", [1, S], BF16) for i in range(2)]
        est = sbt("est", [128, S], BF16)
        estS = sbt("estS", [128, S], BF16)
        aTs = sbt("aTs", [128, 512], BF16)
        tris = sbt("tris", [128, nh * 128], F32)
        aprts = sbt("aprts", [128, nh], F32)
        idb = sbt("idb", [128, 128], BF16)
        idfs = sbt("idfs", [128, 128], F32)
        sq = sbt("sq", [128, 128], BF16)
        ksq = sbt("ksq", [128, 512], BF16)
        rowq = sbt("rowq", [1, 128], F32)
        knrow = sbt("knrow", [1, n_ks * S], F32)
        knrow2 = sbt("knrow2", [1, n_ks * S], F32)
        km = sbt("km", [1, 1], F32)
        kmxb = sbt("kmxb", [128, 1], F32)
        mcol = sbt("mcol", [128, 1], F32)
        mcol2 = sbt("mcol2", [128, 1], F32)
        mcol3 = sbt("mcol3", [128, 1], F32)
        biasc = sbt("biasc", [128, 1], F32)
        lparts = sbt("lparts", [128, 8], F32)
        linv0 = sbt("linv0", [128, 1], F32)
        linv = sbt("linv", [128, 1], F32)

        ensure_consts(em, st)
        dma(em, "gpsimd", idb[:], idf, sem="li")
        dma(em, "sync", idfs[:], idf, sem="li2")
        dma(em, "sync", tris[:], tri, sem="ltr")
        dma(em, "sync", aprts[:], aprt, sem="lap")
        em.flush()

        nkchunks = n_ks * 8
        with ExitStack() as c2:
            pst = lambda n, sh, dt: c2.enter_context(nc.psum_tensor(n, sh, dt))
            sp = pst("sp", [128, 1024], F32)
            atp = pst("atp", [128, 512], BF16)
            pvp = pst("pvp", [128, 128], F32)
            pq = pst("pq", [1, 128], F32)
            pk = pst("pk", [1, 512], F32)
            ptr = pst("ptr", [128, 1], F32)
            pkb = pst("pkb", [128, 1], F32)

            for qb in range(NBLK):
                dma(em, "gpsimd", ksb[:], kq[qb], sem="kld",
                    wait=[("qkc", 8 * nh * qb), ("ksd", nkchunks * qb)] if qb > 0 else None)
                dma(em, "gpsimd", vsb[:], vq[qb], sem="vld",
                    wait=[("pvg", 8 * nh * qb)] if qb > 0 else None)
                for c in range(nkchunks):
                    em.op("scalar", lambda e, c=c: e.activation(
                        ksq[:], ksb[:, c * 512:(c + 1) * 512], AF.Square),
                        wait=[("kld", 16 * (qb + 1)), ("knm", nkchunks * qb + c)],
                        inc=("ksd", 1))
                    em.op("tensor", lambda e, c=c: e.matmul(
                        pk[:], st["ones_col_bf"][:], ksq[:],
                        start=True, stop=True, skip_group_check=True),
                        wait=[("ksd", nkchunks * qb + c + 1), ("knc", nkchunks * qb + c)],
                        inc=("knm", 1))
                    em.op("vector", lambda e, c=c: e.tensor_copy(
                        knrow[:, c * 512:(c + 1) * 512], pk[:]),
                        wait=[("knm", nkchunks * qb + c + 1)], inc=("knc", 1))
                em.op("scalar", lambda e: e.activation(knrow2[:], knrow[:], AF.Sqrt),
                      wait=[("knc", nkchunks * (qb + 1))], inc=("kns", 1))
                em.op("vector", lambda e: e.reduce_max(km[:], knrow2[:], axis=AX.X),
                      wait=[("kns", qb + 1)], inc=("kmr", 1))
                em.op("tensor", lambda e: e.matmul(
                    pkb[:], st["ones_row_f"][:], km[:],
                    start=True, stop=True, skip_group_check=True),
                    wait=[("kmr", qb + 1), ("kbc", qb)], inc=("kbm", 1))
                em.op("vector", lambda e: e.tensor_copy(kmxb[:], pkb[:]),
                      wait=[("kbm", qb + 1)], inc=("kbc", 1))

                for h in range(nh):
                    s = qb * nh + h
                    par = s % 2
                    kvh = h // (nh // nkv)
                    rb2 = 0 if hd == 128 else (kvh % 2) * 64
                    kcb = 0 if n_ks == 1 else kvh // 2
                    ch0 = h * hd
                    ocn = ch0 // 128
                    rbo = ch0 % 128
                    dma(em, "gpsimd", qsb[par][rb2:rb2 + hd, :],
                        qT[h * hd:(h + 1) * hd, qb * 128:(qb + 1) * 128],
                        sem=f"qld{par}",
                        wait=[("qkc", 8 * (s - 1)), ("sqd_", s - 1)] if s >= 2 else None)
                    dma(em, "gpsimd", crb[par][:],
                        crow[qb, h].rearrange("(a u) -> a u", a=1),
                        sem=f"crd{par}", wait=[("qkc", 8 * (s - 1))] if s >= 2 else None)
                    em.op("scalar", lambda e, par=par, rb2=rb2: e.activation(
                        sq[rb2:rb2 + hd, :], qsb[par][rb2:rb2 + hd, :], AF.Square),
                        wait=[(f"qld{par}", 16 * (s // 2 + 1)), ("xpd", 8 * s), ("qnm", s)],
                        inc=("sqd_", 1))
                    em.op("tensor", lambda e, rb2=rb2: e.matmul(
                        pq[:], st["ones_col_bf"][rb2:rb2 + hd, :], sq[rb2:rb2 + hd, :],
                        start=True, stop=True, skip_group_check=True),
                        wait=[("sqd_", s + 1), ("rqc", s)], inc=("qnm", 1))
                    em.op("vector", lambda e: e.tensor_copy(rowq[:], pq[:]),
                          wait=[("qnm", s + 1)], inc=("rqc", 1))
                    em.op("tensor", lambda e: e.transpose(ptr[:], rowq[:], idfs[0:1, 0:1]),
                          wait=[("li2", 16), ("rqc", s + 1), ("mcc", s)], inc=("ptt", 1))
                    em.op("vector", lambda e: e.tensor_copy(mcol[:], ptr[:]),
                          wait=[("ptt", s + 1)], inc=("mcc", 1))
                    em.op("scalar", lambda e: e.activation(mcol2[:], mcol[:], AF.Sqrt),
                          wait=[("mcc", s + 1)], inc=("sqt", 1))
                    em.op("vector", lambda e: e.tensor_mul(mcol3[:], mcol2[:], kmxb[:]),
                          wait=[("sqt", s + 1), ("kbc", qb + 1)])
                    em.op("vector", lambda e, h=h: e.tensor_sub(
                        biasc[:], aprts[:, h:h + 1], mcol3[:]),
                        wait=[("lap", 16)], inc=("bcd", 1))
                    for c in range(8):
                        half = c % 2
                        w = [("xpd", 8 * s + c - 1)] if (8 * s + c - 1) > 0 else []
                        if c == 0:
                            w += [(f"qld{par}", 16 * (s // 2 + 1)),
                                  (f"crd{par}", 16 * (s // 2 + 1)),
                                  ("kld", 16 * (qb + 1))]
                        em.op("tensor", lambda e, par=par, half=half, c=c, rb2=rb2,
                              kcb=kcb: e.matmul(
                            sp[:, half * 512:(half + 1) * 512],
                            qsb[par][rb2:rb2 + hd, :],
                            ksb[rb2:rb2 + hd, kcb * S + c * 512:kcb * S + c * 512 + 512],
                            start=True, stop=False, skip_group_check=True),
                            wait=w)
                        em.op("tensor", lambda e, par=par, half=half, c=c: e.matmul(
                            sp[:, half * 512:(half + 1) * 512],
                            st["ones_row_bf"][:, 0:128],
                            crb[par][:, c * 512:(c + 1) * 512],
                            start=False, stop=True, skip_group_check=True),
                            inc=("qkc", 1))
                        if c == 7:
                            em.op("vector", lambda e, half=half, h=h: e.tensor_add(
                                sp[:, half * 512 + 384:half * 512 + 512],
                                sp[:, half * 512 + 384:half * 512 + 512],
                                tris[:, h * 128:(h + 1) * 128]),
                                wait=[("ltr", 16), ("qkc", 8 * s + c + 1)],
                                inc=("msk", 1))
                        ew = [("qkc", 8 * s + c + 1), ("bcd", s + 1)]
                        if c == 7:
                            ew.append(("msk", s + 1))
                        if c == 0:
                            ew.append(("lrd", s))
                            ew.append(("scd", s))
                        em.op("scalar", lambda e, half=half, c=c: e.activation(
                            est[:, c * 512:(c + 1) * 512],
                            sp[:, half * 512:(half + 1) * 512], AF.Exp,
                            bias=biasc[:], accum_out=lparts[:, c:c + 1]),
                            wait=ew, inc=("xpd", 1))
                    em.op("vector", lambda e: e.reduce_sum(linv0[:], lparts[:], axis=AX.X),
                          wait=[("xpd", 8 * (s + 1))])
                    em.op("vector", lambda e: e.reciprocal(linv[:], linv0[:]),
                          inc=("lrd", 1))
                    em.op("scalar", lambda e: e.activation(estS[:], est[:], AF.Copy,
                                                           scale=linv[:]),
                          wait=[("lrd", s + 1), ("trp", 8 * s)], inc=("scd", 1))
                    for g in range(8):
                        for i4 in range(4):
                            t = g * 4 + i4
                            em.op("tensor", lambda e, t=t, i4=i4: e.transpose(
                                atp[:, i4 * 128:(i4 + 1) * 128],
                                estS[:, t * 128:(t + 1) * 128], idb[:]),
                                wait=[("li", 16), ("scd", s + 1), ("acp", 8 * s + g)]
                                if i4 == 0 else None,
                                inc=("trp", 1) if i4 == 3 else None)
                        em.op("vector", lambda e: e.tensor_copy(aTs[:], atp[:]),
                              wait=[("trp", 8 * s + g + 1), ("pvg", 8 * s + g)],
                              inc=("acp", 1))
                        for i4 in range(4):
                            t = g * 4 + i4
                            em.op("tensor", lambda e, t=t, i4=i4, kvh=kvh, rbo=rbo: e.matmul(
                                pvp[rbo:rbo + hd, :],
                                vsb[:, t * vw + kvh * hd:t * vw + kvh * hd + hd],
                                aTs[:, i4 * 128:(i4 + 1) * 128],
                                start=(t == 0), stop=(t == 31), skip_group_check=True,
                                tile_position=(0, rbo)),
                                wait=([("acp", 8 * s + g + 1)] +
                                      ([("sev", s), ("vld", 16 * (qb + 1))]
                                       if g == 0 else []))
                                if i4 == 0 else None,
                                inc=("pvg", 1) if i4 == 3 else None)
                    em.op("vector", lambda e, ocn=ocn, rbo=rbo, qb=qb: e.tensor_copy(
                        attn_sh[rbo:rbo + hd, ocn * T + qb * 128: ocn * T + qb * 128 + 128],
                        pvp[rbo:rbo + hd, :]),
                        wait=[("pvg", 8 * (s + 1))], inc=("sev", 1))
                em.flush()
            em.op("vector", wait=[("sev", nh * NBLK)], chain=False)
            em.flush()
        if dbg is not None:
            dma(em, "gpsimd", dbg["ao"], attn_sh[:, 0:T], sem="dao")
            em.op("gpsimd", wait=[("dao", 16)], chain=False)
            em.flush()

        proj_residual_ln(em, st, "o", woTt, attn_sh, xT, xoutT,
                         first_wait_extra=[("sev", nh * NBLK)], dbg=dbg,
                         kc_in=kc_out)
    return nc


def build_l1_attn(debug=False):
    return build_attn(32, 128, 1, debug=debug)


def build_l0_attn(debug=False):
    return build_attn(16, 64, 4, debug=debug)



def alibi_slopes(n_heads=32):
    closest = 2 ** int(math.floor(math.log2(n_heads)))
    base = 2.0 ** (-(2.0 ** (-(math.log2(closest) - 3))))
    slopes = [base ** i for i in range(1, closest + 1)]
    if closest != n_heads:
        eb = 2.0 ** (-(2.0 ** (-(math.log2(2 * closest) - 3))))
        ne = min(closest, n_heads - closest)
        slopes += [eb ** i for i in range(1, 2 * ne + 1, 2)]
    return np.asarray(slopes, np.float64)


def host_attn_core(c, kT_full, V_full, nh, nkv, hd, slopes):
    n_ks = (nkv * hd) // 128
    vw = nkv * hd
    kq = np.zeros((NBLK, 128, n_ks * S), np.float32)
    vq = np.zeros((NBLK, 128, 32 * vw), np.float32)
    crow = np.full((NBLK, nh, S), -1e30, np.float32)
    for qb, g in enumerate(own_blocks(c)):
        b, j = blk_batch_j(g)
        qbase = 128 * j
        for kvh in range(nkv):
            rb2 = 0 if hd == 128 else (kvh % 2) * 64
            kcb = 0 if n_ks == 1 else kvh // 2
            rows = kT_full[b][kvh * hd:(kvh + 1) * hd, :]
            kq[qb, rb2:rb2 + hd, kcb * S:kcb * S + qbase] = rows[:, 0:qbase]
            kq[qb, rb2:rb2 + hd, kcb * S + S - 128:kcb * S + S] = rows[:, qbase:qbase + 128]
        for t in range(qbase // 128):
            vq[qb, :, t * vw:(t + 1) * vw] = V_full[b][t * 128:(t + 1) * 128, :]
        vq[qb, :, 31 * vw:32 * vw] = V_full[b][qbase:qbase + 128, :]
        for h in range(nh):
            if qbase > 0:
                crow[qb, h, 0:qbase] = slopes[h] * (np.arange(qbase) - qbase)
            crow[qb, h, S - 128:S] = 0.0
    return kq, vq, crow


def host_attn_tri(nh, slopes):
    p = np.arange(128)[:, None]
    jt = np.arange(128)[None, :]
    out = np.empty((128, nh * 128), np.float32)
    for h in range(nh):
        out[:, h * 128:(h + 1) * 128] = np.where(
            jt <= p, slopes[h] * jt, -1e30).astype(np.float32)
    return out


def host_attn_aprt(nh, slopes):
    return np.ascontiguousarray(
        (-(np.asarray(slopes)[None, :nh]) * np.arange(128)[:, None]).astype(np.float32))


def host_l1_attn_core(c, kT_full, V_full):
    return host_attn_core(c, kT_full, V_full, 32, 1, 128, alibi_slopes(32))


def host_l1_tri():
    return host_attn_tri(32, alibi_slopes(32))


def host_l1_aprt():
    return host_attn_aprt(32, alibi_slopes(32))


def build_l0_qkv():
    nc = bass.Bass("TRN2", target_bir_lowering=False, debug=False)
    xT = nc.dram_tensor("xT", [D, T], F32, kind="ExternalInput").ap()
    xTh = nc.dram_tensor("xTh", [128, KC * NBLK * HALO], F32, kind="ExternalInput").ap()
    wqkvTt = nc.dram_tensor("wqkvTt", [12, 128, KC * 128], F32, kind="ExternalInput").ap()
    qc1w = nc.dram_tensor("qc1w", [128, 3 * 1024], F32, kind="ExternalInput").ap()
    qc2w = nc.dram_tensor("qc2w", [128, 3 * 8 * 8 * 128], F32, kind="ExternalInput").ap()
    kc1w = nc.dram_tensor("kc1w", [128, 3 * 256], F32, kind="ExternalInput").ap()
    kc2w = nc.dram_tensor("kc2w", [128, 3 * 2 * 2 * 128], F32, kind="ExternalInput").ap()
    qc1b = nc.dram_tensor("qc1b", [1024], F32, kind="ExternalInput").ap()
    qc2b = nc.dram_tensor("qc2b", [1024], F32, kind="ExternalInput").ap()
    kc1b = nc.dram_tensor("kc1b", [256], F32, kind="ExternalInput").ap()
    kc2b = nc.dram_tensor("kc2b", [256], F32, kind="ExternalInput").ap()
    ktm = nc.dram_tensor("ktm", [1], F32, kind="ExternalInput").ap()
    ropeC = nc.dram_tensor("ropeC", [128, T], F32, kind="ExternalInput").ap()
    ropeS = nc.dram_tensor("ropeS", [128, T], F32, kind="ExternalInput").ap()
    ropePT = nc.dram_tensor("ropePT", [128, 128], F32, kind="ExternalInput").ap()
    emq = nc.dram_tensor("emq", [128, 128], F32, kind="ExternalInput").ap()
    emk = nc.dram_tensor("emk", [128, 128], F32, kind="ExternalInput").ap()
    idf = nc.dram_tensor("idf", [128, 128], F32, kind="ExternalInput").ap()
    qT = nc.dram_tensor("qT", [1024, T], F32, kind="ExternalOutput").ap()
    kT = nc.dram_tensor("kT", [256, T], F32, kind="ExternalOutput").ap()
    V = nc.dram_tensor("V", [T, 256], F32, kind="ExternalOutput").ap()

    with ExitStack() as ctx:
        em = Em(nc, ctx)
        st = {}
        sbt = lambda n, sh, dt: ctx.enter_context(nc.sbuf_tensor(n, sh, dt))
        shadow = sbt("shadow", [128, KC * T], BF16)
        shadowH = sbt("shadowH", [128, KC * NBLK * HALO], BF16)
        qkvT = sbt("qkvT", [128, 12 * T], BF16)
        qkvH = sbt("qkvH", [128, 12 * NBLK * HALO], BF16)
        wbuf = sbt("wbuf", [128, KC * 128], BF16)
        q1wS = sbt("q1wS", [128, 3 * 1024], BF16)
        q2wS = sbt("q2wS", [128, 3 * 8 * 8 * 128], BF16)
        k1wS = sbt("k1wS", [128, 3 * 256], BF16)
        k2wS = sbt("k2wS", [128, 3 * 2 * 2 * 128], BF16)
        q1bS = sbt("q1bS", [1, 1024], BF16)
        q2bS = sbt("q2bS", [1, 1024], BF16)
        k1bS = sbt("k1bS", [1, 256], BF16)
        k2bS = sbt("k2bS", [1, 256], BF16)
        ktS = sbt("ktS", [1, 1], F32)
        rcS = sbt("rcS", [128, T], F32)
        rsS = sbt("rsS", [128, T], F32)
        rpS = sbt("rpS", [128, 128], BF16)
        emqS = sbt("emqS", [128, 128], BF16)
        emkS = sbt("emkS", [128, 128], BF16)
        idb = sbt("idb", [128, 128], BF16)
        cin = sbt("cin", [128, 8 * 132], BF16)
        cin2 = sbt("cin2", [128, 2 * 132], BF16)
        q1 = sbt("q1", [128, 8 * 130], BF16)
        k1 = sbt("k1", [128, 2 * 130], BF16)
        q2f = sbt("q2f", [128, 8 * 128], F32)
        k2f = sbt("k2f", [128, 2 * 128], F32)
        qxf = sbt("qxf", [128, 8 * 128], F32)
        kxf = sbt("kxf", [128, 2 * 128], F32)
        sqb = sbt("sqb", [128, 128], BF16)
        nrow = sbt("nrow", [1, 128], F32)
        nrow2 = sbt("nrow2", [1, 128], F32)
        qnf = sbt("qnf", [128, 8 * 128], F32)
        knf = sbt("knf", [128, 2 * 128], F32)
        qnb = sbt("qnb", [128, 128], BF16)
        t1b = sbt("t1b", [128, 128], F32)
        t2b = sbt("t2b", [128, 128], F32)
        qfb = sbt("qfb", [128, 128], F32)
        vtb = sbt("vtb", [128, 256], F32)
        vsh = sbt("vsh", [128, 128], BF16)

        ensure_consts(em, st)
        load_shadow(em, shadow, xT, "l")
        dma(em, "gpsimd", shadowH[:], xTh, sem="lhh")
        for nm, dst, srcp in [("qc1w", q1wS, qc1w), ("qc2w", q2wS, qc2w),
                              ("kc1w", k1wS, kc1w), ("kc2w", k2wS, kc2w),
                              ("ropePT", rpS, ropePT), ("emq", emqS, emq),
                              ("emk", emkS, emk), ("idf", idb, idf)]:
            dma(em, "gpsimd", dst[:], srcp, sem="lw1")
        for nm, dst, srcp in [("qc1b", q1bS, qc1b), ("qc2b", q2bS, qc2b),
                              ("kc1b", k1bS, kc1b), ("kc2b", k2bS, kc2b)]:
            dma(em, "gpsimd", dst[:], srcp.rearrange("(a o) -> a o", a=1), sem="lw1")
        dma(em, "sync", ktS[:], ktm.rearrange("(a o) -> a o", a=1), sem="lw2")
        dma(em, "sync", rcS[:], ropeC, sem="lw2")
        dma(em, "sync", rsS[:], ropeS, sem="lw2")
        em.flush()

        with ExitStack() as c2:
            qp = c2.enter_context(nc.psum_tensor("qp", [128, T], F32))
            hp = c2.enter_context(nc.psum_tensor("hp", [128, NBLK * HALO], F32))
            HB = NBLK * HALO
            for j in range(12):
                em.op_dma("gpsimd", lambda e, j=j: e.dma_start(out=wbuf[:], in_=wqkvTt[j]),
                          wait=[("pm", j)] if j > 0 else None, inc=("pw", 16))
                th = []
                for h in range(2):
                    for k in range(KC):
                        th.append(lambda e, h=h, k=k: e.matmul(
                            qp[:, h * 512:(h + 1) * 512],
                            wbuf[:, k * 128:(k + 1) * 128],
                            shadow[:, k * T + h * 512:k * T + h * 512 + 512],
                            start=(k == 0), stop=False, skip_group_check=True))
                for k in range(KC):
                    th.append(lambda e, k=k: e.matmul(
                        hp[:], wbuf[:, k * 128:(k + 1) * 128],
                        shadowH[:, k * HB:(k + 1) * HB],
                        start=(k == 0), stop=False, skip_group_check=True))
                fw = [("pw", 16 * (j + 1)), ("lsh", em.val("lsh")), ("lhh", 16)]
                if j > 0:
                    fw.append(("pe", j))
                emit_group(em, "tensor", th, first_wait=fw, last_inc=("pm", 1))
                em.op("vector", lambda e, j=j: e.tensor_copy(qkvT[:, j * T:(j + 1) * T], qp[:]),
                      wait=[("pm", j + 1)])
                em.op("vector", lambda e, j=j: e.tensor_copy(
                    qkvH[:, j * HB:(j + 1) * HB], hp[:]),
                    inc=("pe", 1))
            em.op("vector", wait=[("pe", 12)], chain=False)
            em.flush()

        with ExitStack() as c3:
            c1p = c3.enter_context(nc.psum_tensor("c1p", [64, 130], F32))
            c2p = c3.enter_context(nc.psum_tensor("c2p", [128, 128], F32))
            nrp = c3.enter_context(nc.psum_tensor("nrp", [1, 128], F32))
            bcp = c3.enter_context(nc.psum_tensor("bcp", [128, 128], F32))
            sqp = c3.enter_context(nc.psum_tensor("sqp", [128, 128], F32))
            vtp = c3.enter_context(nc.psum_tensor("vtp", [128, 128], BF16))
            qmp = c3.enter_context(nc.psum_tensor("qmp", [128, 128], F32))
            kmp = c3.enter_context(nc.psum_tensor("kmp", [128, 128], F32))
            HB = NBLK * HALO

            for b in range(NBLK):
                for m in range(8):
                    em.op("vector", lambda e, m=m, b=b: e.tensor_copy(
                        cin[:, m * 132:m * 132 + 4],
                        qkvH[:, m * HB + b * HALO:m * HB + b * HALO + 4]))
                    em.op("vector", lambda e, m=m, b=b: e.tensor_copy(
                        cin[:, m * 132 + 4:m * 132 + 132],
                        qkvT[:, m * T + b * 128:m * T + b * 128 + 128]))
                for m in range(2):
                    em.op("vector", lambda e, m=m, b=b: e.tensor_copy(
                        cin2[:, m * 132:m * 132 + 4],
                        qkvH[:, (8 + m) * HB + b * HALO:(8 + m) * HB + b * HALO + 4]))
                    em.op("vector", lambda e, m=m, b=b: e.tensor_copy(
                        cin2[:, m * 132 + 4:m * 132 + 132],
                        qkvT[:, (8 + m) * T + b * 128:(8 + m) * T + b * 128 + 128]),
                        inc=("cina", 1) if m == 1 else None)

                th = [lambda e, m=m2, b=b: e.matmul(
                    qmp[:], emqS[:], qkvT[:, m * T + b * 128:m * T + b * 128 + 128],
                    start=(m == 0), stop=(m == 7), skip_group_check=True)
                    for m2 in range(8)]
                th += [lambda e, m=m2, b=b: e.matmul(
                    kmp[:], emkS[:], qkvT[:, (8 + m) * T + b * 128:(8 + m) * T + b * 128 + 128],
                    start=(m == 0), stop=(m == 1), skip_group_check=True)
                    for m2 in range(2)]
                emit_group(em, "tensor", th, first_wait=[("lw1", em.val("lw1")),
                                                         ("mixd", em.val("mixd"))],
                           last_inc=("meanm", 1))

                def c1group(g, src, wsb, bsb, dst, wdt):
                    rb = (g % 2) * 64
                    th = [lambda e, dt=dt, g=g, rb=rb: e.matmul(
                        c1p[:], wsb[rb:rb + 64, dt * wdt + g * 64:dt * wdt + g * 64 + 64],
                        src[rb:rb + 64, (g // 2) * 132 + dt:(g // 2) * 132 + dt + 130],
                        start=(dt == 0), stop=False, skip_group_check=True)
                        for dt in range(3)]
                    th.append(lambda e, g=g: e.matmul(
                        c1p[:], bsb[:, g * 64:g * 64 + 64], st["ones_row_bf"][:, 0:130],
                        start=False, stop=True, skip_group_check=True))
                    emit_group(em, "tensor", th,
                               first_wait=[("cina", em.val("cina")),
                                           ("c1e", em.val("c1e")),
                                           ("lw1", em.val("lw1"))],
                               last_inc=("c1m", 1))
                    em.op("vector", lambda e, g=g, rb=rb: e.tensor_copy(
                        dst[rb:rb + 64, (g // 2) * 130:(g // 2) * 130 + 130], c1p[:]),
                        wait=[("c1m", em.val("c1m"))], inc=("c1e", 1))

                for g in range(16):
                    c1group(g, cin, q1wS, q1bS, q1, 1024)
                for g in range(4):
                    c1group(g, cin2, k1wS, k1bS, k1, 256)

                def c2chunk(coc, src, wsb, bsb, dst, ncic, stride):
                    th = []
                    n = 0
                    for dt in range(3):
                        for cic in range(ncic):
                            th.append(lambda e, dt=dt, cic=cic, coc=coc, n=n: e.matmul(
                                c2p[:], wsb[:, ((dt * ncic + cic) * stride + coc) * 128:
                                            ((dt * ncic + cic) * stride + coc) * 128 + 128],
                                src[:, cic * 130 + dt:cic * 130 + dt + 128],
                                start=(n == 0), stop=False, skip_group_check=True))
                            n += 1
                    th.append(lambda e, coc=coc: e.matmul(
                        c2p[:], bsb[:, coc * 128:coc * 128 + 128],
                        st["ones_row_bf"][:, 0:128],
                        start=False, stop=True, skip_group_check=True))
                    emit_group(em, "tensor", th,
                               first_wait=[("c1e", em.val("c1e")),
                                           ("c2e", em.val("c2e")),
                                           ("lw1", em.val("lw1"))],
                               last_inc=("c2m", 1))
                    em.op("vector", lambda e, coc=coc: e.tensor_copy(
                        dst[:, coc * 128:(coc + 1) * 128], c2p[:]),
                        wait=[("c2m", em.val("c2m"))], inc=("c2e", 1))

                for coc in range(8):
                    c2chunk(coc, q1, q2wS, q2bS, q2f, 8, 8)
                for coc in range(2):
                    c2chunk(coc, k1, k2wS, k2bS, k2f, 2, 2)

                for m in range(8):
                    em.op("vector", lambda e, m=m: e.tensor_add(
                        qxf[:, m * 128:(m + 1) * 128], q2f[:, m * 128:(m + 1) * 128],
                        kmp[:]),
                        wait=[("meanm", em.val("meanm")), ("c2e", em.val("c2e"))]
                        if m == 0 else None)
                for m in range(2):
                    em.op("vector", lambda e, m=m: e.tensor_add(
                        kxf[:, m * 128:(m + 1) * 128], k2f[:, m * 128:(m + 1) * 128],
                        qmp[:]),
                        inc=("mixd", 1) if m == 1 else None)

                def l2n(h, src, dst, qscale):
                    rb = (h % 2) * 64
                    cs = (h // 2) * 128
                    em.op("scalar", lambda e, rb=rb, cs=cs: e.activation(
                        sqb[rb:rb + 64, 0:128], src[rb:rb + 64, cs:cs + 128], AF.Square),
                        wait=[("mixd", em.val("mixd")), ("nrm", em.val("nrm"))],
                        inc=("sqk", 1))
                    em.op("tensor", lambda e, rb=rb: e.matmul(
                        nrp[:], st["ones_col_bf"][rb:rb + 64, :], sqb[rb:rb + 64, 0:128],
                        start=True, stop=True, skip_group_check=True),
                        wait=[("sqk", em.val("sqk")), ("nrc", em.val("nrc"))],
                        inc=("nrm", 1))
                    em.op("vector", lambda e: e.tensor_copy(nrow[:], nrp[:]),
                          wait=[("nrm", em.val("nrm"))], inc=("nrc", 1))
                    em.op("scalar", lambda e: e.activation(nrow2[:], nrow[:], AF.Sqrt),
                          wait=[("nrc", em.val("nrc"))], inc=("nsq", 1))
                    em.op("vector", lambda e: e.tensor_scalar_max(nrow2[:], nrow2[:], 1e-12),
                          wait=[("nsq", em.val("nsq"))])
                    em.op("vector", lambda e: e.reciprocal(nrow[:], nrow2[:]))
                    if qscale is None:
                        em.op("vector", lambda e: e.tensor_single_scalar(
                            nrow[:], nrow[:], ktS[0:1, 0:1], ALU.mult),
                            wait=[("lw2", em.val("lw2"))])
                    else:
                        em.op("vector", lambda e, qs=qscale: e.tensor_scalar_mul(
                            nrow[:], nrow[:], qs))
                    em.op("vector", fn=None, inc=("nrr", 1))
                    em.op("tensor", lambda e, rb=rb: e.matmul(
                        bcp[rb:rb + 64, :], st["ones_row_f"][:, 0:64], nrow[:],
                        start=True, stop=True, skip_group_check=True,
                        tile_position=(0, rb)),
                        wait=[("nrr", em.val("nrr")), ("nbe", em.val("nbe"))],
                        inc=("nbm", 1))
                    em.op("vector", lambda e, rb=rb, cs=cs: e.tensor_mul(
                        dst[rb:rb + 64, cs:cs + 128],
                        src[rb:rb + 64, cs:cs + 128], bcp[rb:rb + 64, :]),
                        wait=[("nbm", em.val("nbm"))], inc=("nbe", 1))

                for h in range(16):
                    l2n(h, qxf, qnf, 0.125)
                for h in range(4):
                    l2n(h, kxf, knf, None)

                def rope_tile(src, ti, out_dram, last=False):
                    em.op("vector", lambda e, ti=ti: e.tensor_copy(
                        qnb[:], src[:, ti * 128:(ti + 1) * 128]),
                        wait=[("nbe", em.val("nbe"))], inc=("rqb", 1))
                    em.op("tensor", lambda e: e.matmul(
                        sqp[:], rpS[:], qnb[:], start=True, stop=True,
                        skip_group_check=True),
                        wait=[("rqb", em.val("rqb")), ("rt2", em.val("rt2"))],
                        inc=("rpm", 1))
                    em.op("vector", lambda e, ti=ti, b=b: e.tensor_mul(
                        t1b[:], src[:, ti * 128:(ti + 1) * 128],
                        rcS[:, b * 128:b * 128 + 128]),
                        wait=[("lw2", em.val("lw2"))])
                    em.op("vector", lambda e, b=b: e.tensor_mul(
                        t2b[:], sqp[:], rsS[:, b * 128:b * 128 + 128]),
                        wait=[("rpm", em.val("rpm"))], inc=("rt2", 1))
                    em.op("vector", lambda e: e.tensor_add(qfb[:], t1b[:], t2b[:]),
                          inc=("rfd", 1))
                    em.op_dma("sync", lambda e, out_dram=out_dram: e.dma_start(
                        out=out_dram, in_=qfb[:]),
                        wait=[("rfd", em.val("rfd"))], inc=("rst", 16))
                    em.op("vector", fn=None, wait=[("rst", em.val("rst"))], chain=False)

                for ti in range(8):
                    rope_tile(qnf, ti, qT[ti * 128:(ti + 1) * 128, b * 128:b * 128 + 128])
                for ti in range(2):
                    rope_tile(knf, ti, kT[ti * 128:(ti + 1) * 128, b * 128:b * 128 + 128])

                em.op("vector", lambda e, b=b: e.tensor_copy(
                    vsh[:, 0:1], qkvH[:, 11 * HB + b * HALO + 3:11 * HB + b * HALO + 4]))
                em.op("vector", lambda e, b=b: e.tensor_copy(
                    vsh[:, 1:128], qkvT[:, 11 * T + b * 128:11 * T + b * 128 + 127]),
                    inc=("vsa", 1))
                for part in range(2):
                    if part == 0:
                        th_tr = lambda e, b=b: e.transpose(
                            vtp[:], qkvT[:, 10 * T + b * 128:10 * T + b * 128 + 128], idb[:])
                    else:
                        th_tr = lambda e: e.transpose(vtp[:], vsh[:], idb[:])
                    em.op("tensor", th_tr,
                          wait=[("vsa", em.val("vsa")), ("vte", em.val("vte")),
                                ("lw1", em.val("lw1"))],
                          inc=("vtm", 1))
                    em.op("vector", lambda e, part=part: e.tensor_copy(
                        vtb[:, part * 128:(part + 1) * 128], vtp[:]),
                        wait=[("vtm", em.val("vtm"))], inc=("vte", 1))
                em.op_dma("sync", lambda e, b=b: e.dma_start(
                    out=V[b * 128:(b + 1) * 128, :], in_=vtb[:]),
                    wait=[("vte", em.val("vte"))], inc=("vst", 16))
                em.op("vector", fn=None, wait=[("vst", em.val("vst"))], chain=False)
                em.flush()
        em.op("sync", wait=[("rst", em.val("rst")), ("vst", em.val("vst"))])
        em.flush()
    return nc



def host_l0_conv_weights(inputs):
    qc1 = np.asarray(inputs["L0_attn_qc1_w"])
    kc1 = np.asarray(inputs["L0_attn_kc1_w"])
    qc2 = np.asarray(inputs["L0_attn_qc2_w"])
    kc2 = np.asarray(inputs["L0_attn_kc2_w"])

    def c1h(w, nco):
        out = np.zeros((128, 3 * nco), np.float32)
        for dt in range(3):
            blk = np.ascontiguousarray(w[:, :, dt].T)
            out[0:64, dt * nco:(dt + 1) * nco] = blk
            out[64:128, dt * nco:(dt + 1) * nco] = blk
        return out

    def c2h(w, ncc):
        out = np.zeros((128, 3 * ncc * ncc * 128), np.float32)
        for dt in range(3):
            for cic in range(ncc):
                for coc in range(ncc):
                    j = ((dt * ncc + cic) * ncc + coc) * 128
                    out[:, j:j + 128] = w[coc * 128:(coc + 1) * 128,
                                          cic * 128:(cic + 1) * 128, dt].T
        return out

    return c1h(qc1, 1024), c2h(qc2, 8), c1h(kc1, 256), c2h(kc2, 2)


def host_rope_tables(c):
    inv = 1.0 / (10000.0 ** (np.arange(0, 64, 2, dtype=np.float64) / 64))
    poss = []
    for g in own_blocks(c):
        b, j = blk_batch_j(g)
        poss.append(np.arange(128 * j, 128 * j + 128))
    pos = np.concatenate(poss).astype(np.float64)
    ang = pos[:, None] * inv[None, :]
    C = np.cos(ang).T
    Sn = np.sin(ang).T
    rows_i = (np.arange(128) % 64) // 2
    return (C[rows_i, :].astype(np.float32), Sn[rows_i, :].astype(np.float32))


def host_rope_perm():
    P = np.zeros((64, 64), np.float32)
    for i in range(32):
        P[2 * i, 2 * i + 1] = -1.0
        P[2 * i + 1, 2 * i] = 1.0
    P2 = np.zeros((128, 128), np.float32)
    P2[0:64, 0:64] = P
    P2[64:128, 64:128] = P
    return np.ascontiguousarray(P2.T)


def host_emean():
    emq = np.zeros((128, 128), np.float32)
    emk = np.zeros((128, 128), np.float32)
    p = np.arange(128)
    for d in range(128):
        emq[p % 64 == d % 64, d] = 0.5 / 16
        emk[p % 64 == d % 64, d] = 0.5 / 4
    return emq, emk


def host_xT_halo(x_full, c):
    out = np.zeros((128, KC * NBLK * HALO), np.float32)
    for bi, g in enumerate(own_blocks(c)):
        b, j = blk_batch_j(g)
        t0 = 128 * j
        for t in range(HALO):
            tok = t0 - HALO + t
            if tok < 0:
                continue
            col = x_full[b, tok, :]
            for kc in range(KC):
                out[:, kc * (NBLK * HALO) + bi * HALO + t] = col[kc * 128:(kc + 1) * 128]
    return out


def _assemble_kv(kT_p, V_p, krows, vcols):
    kT_full, V_full = [], []
    for b in range(B):
        kc_ = np.zeros((krows, S), np.float32)
        vr = np.zeros((S, vcols), np.float32)
        for c in range(NCORES):
            for i, g in enumerate(own_blocks(c)):
                bb, j = blk_batch_j(g)
                if bb != b:
                    continue
                kc_[:, 128 * j:128 * j + 128] = kT_p[c][:, 128 * i:128 * i + 128]
                vr[128 * j:128 * j + 128, :] = V_p[c][128 * i:128 * i + 128, :]
        kT_full.append(kc_)
        V_full.append(vr)
    return kT_full, V_full


def kernel(**inputs):
    inp = {k: np.asarray(v, dtype=np.float32) for k, v in inputs.items()}
    x = inp["L0_attn_x"]
    parts = shard_xT(x)
    idf = np.eye(128, dtype=np.float32)
    z16 = np.zeros(16, np.float64)
    sl32 = alibi_slopes(32)

    nc = build_l0_qkv()
    wqkvTt = tile_lhsT(np.ascontiguousarray(inp["L0_attn_w_qkv"].T))
    qc1h, qc2h, kc1h, kc2h = host_l0_conv_weights(inp)
    emq, emk = host_emean()
    base = {"wqkvTt": wqkvTt, "qc1w": qc1h, "qc2w": qc2h, "kc1w": kc1h,
            "kc2w": kc2h, "qc1b": inp["L0_attn_qc1_b"], "qc2b": inp["L0_attn_qc2_b"],
            "kc1b": inp["L0_attn_kc1_b"], "kc2b": inp["L0_attn_kc2_b"],
            "ktm": inp["L0_attn_key_temp"], "ropePT": host_rope_perm(),
            "emq": emq, "emk": emk, "idf": idf}

    def in1(c):
        m = dict(base)
        m["xT"] = parts[c]
        m["xTh"] = host_xT_halo(x, c)
        m["ropeC"], m["ropeS"] = host_rope_tables(c)
        return m

    res = run_bass_kernel_spmd(nc, [in1(c) for c in range(NCORES)],
                               core_ids=list(range(NCORES)))
    qT_p = [res.results[c]["qT"] for c in range(NCORES)]
    kT_full, V_full = _assemble_kv([res.results[c]["kT"] for c in range(NCORES)],
                                   [res.results[c]["V"] for c in range(NCORES)],
                                   256, 256)

    nc = build_l0_attn()
    woTt = tile_lhsT(np.ascontiguousarray(inp["L0_attn_w_o"].T))
    tri = host_attn_tri(16, z16)
    aprt = host_attn_aprt(16, z16)

    def in2(c):
        kqa, vqa, crow = host_attn_core(c, kT_full, V_full, 16, 4, 64, z16)
        return {"qT": qT_p[c], "kq": kqa, "vq": vqa, "crow": crow, "aprt": aprt,
                "tri": tri, "idf": idf, "xT": parts[c], "woTt": woTt}

    res = run_bass_kernel_spmd(nc, [in2(c) for c in range(NCORES)],
                               core_ids=list(range(NCORES)))
    x1_p = [res.results[c]["xoutT"] for c in range(NCORES)]

    nc = build_l0_ffn()
    wTt = tile_lhsT(np.ascontiguousarray(inp["L0_ffn_W"].T))
    aTt = np.ascontiguousarray(
        inp["L0_ffn_lora_A"].transpose(2, 0, 1).reshape(KC, 128, 128)
        .transpose(1, 0, 2).reshape(128, KC * 128))
    bT = np.ascontiguousarray(inp["L0_ffn_lora_B"].transpose(0, 2, 1).reshape(128, D))
    in_maps = [{"xT": x1_p[c], "wTt": wTt, "aTt": aTt, "bT": bT,
                "bias": inp["L0_ffn_bias"], "lw": inp["L0_ffn_lora_weights"],
                "e8": host_e8(2.0)} for c in range(NCORES)]
    res = run_bass_kernel_spmd(nc, in_maps, core_ids=list(range(NCORES)))
    x2_p = [res.results[c]["xoutT"] for c in range(NCORES)]

    nc = build_l1_qkv()
    wqTt = tile_lhsT(inp["L1_attn_Wq"])
    wkvTt = tile_lhsT(inp["L1_attn_Wkv"])
    in_maps = [{"xT": x2_p[c], "wqTt": wqTt, "wkvTt": wkvTt, "idf": idf}
               for c in range(NCORES)]
    res = run_bass_kernel_spmd(nc, in_maps, core_ids=list(range(NCORES)))
    qT_p = [res.results[c]["qT"] for c in range(NCORES)]
    kT_full, V_full = _assemble_kv([res.results[c]["kT"] for c in range(NCORES)],
                                   [res.results[c]["V"] for c in range(NCORES)],
                                   128, 128)

    nc = build_l1_attn()
    woTt = tile_lhsT(inp["L1_attn_Wo"])
    tri = host_attn_tri(32, sl32)
    aprt = host_attn_aprt(32, sl32)

    def in5(c):
        kqa, vqa, crow = host_attn_core(c, kT_full, V_full, 32, 1, 128, sl32)
        return {"qT": qT_p[c], "kq": kqa, "vq": vqa, "crow": crow, "aprt": aprt,
                "tri": tri, "idf": idf, "xT": x2_p[c], "woTt": woTt}

    res = run_bass_kernel_spmd(nc, [in5(c) for c in range(NCORES)],
                               core_ids=list(range(NCORES)))
    x3_p = [res.results[c]["xoutT"] for c in range(NCORES)]

    nc = build_l1_ffn()
    wTt = tile_lhsT(np.ascontiguousarray(inp["L1_ffn_W_base"].T))
    aTt = np.ascontiguousarray(
        inp["L1_ffn_lora_A"].transpose(2, 0, 1).reshape(KC, 128, 128)
        .transpose(1, 0, 2).reshape(128, KC * 128))
    bT = np.ascontiguousarray(inp["L1_ffn_lora_B"].transpose(0, 2, 1).reshape(128, D))
    in_maps = [{"xT": x3_p[c], "wTt": wTt, "aTt": aTt, "bT": bT,
                "bias": inp["L1_ffn_b_base"],
                "wrT": host_router_tiles(inp["L1_ffn_W_router"]),
                "e8f": host_e8(2.0), "idf": idf} for c in range(NCORES)]
    res = run_bass_kernel_spmd(nc, in_maps, core_ids=list(range(NCORES)))
    out_p = [res.results[c]["xoutT"] for c in range(NCORES)]

    return unshard_xT(out_p)
